# version 01
# baseline (speedup 1.0000x reference)
import math
from contextlib import ExitStack

import numpy as np
import ml_dtypes
import concourse.bass as bass
import concourse.mybir as mybir
from concourse.bass_utils import run_bass_kernel_spmd

F32 = mybir.dt.float32
BF16 = mybir.dt.bfloat16
AF = mybir.ActivationFunctionType
ALU = mybir.AluOpType
BF = ml_dtypes.bfloat16

B, S, E, H, D = 4, 4096, 1024, 16, 64
NT = B * S
NC = 8
TPC = NT // NC
SPC = S // NC
EPS = 1e-5
CH = 128
NCHUNK = S // CH


class Em:
    ENGS = ("sync", "tensor", "vector", "scalar", "gpsimd")

    def __init__(self, nc):
        self.nc = nc
        self.items = {e: [] for e in self.ENGS}
        self.cnt = {}

    def count(self, sem):
        return self.cnt.get(id(sem), 0)

    def bump(self, sem, n=1):
        self.cnt[id(sem)] = self.count(sem) + n
        return self.cnt[id(sem)]

    def wait(self, eng, sem, val=None):
        v = self.count(sem) if val is None else val
        self.items[eng].append(lambda Eg, s=sem, vv=v: Eg.wait_ge(s, vv))
        return v

    def do(self, eng, fn, inc=None):
        if inc is None:
            self.items[eng].append(fn)
        else:
            sem, n = inc
            self.bump(sem, n)
            self.items[eng].append(
                lambda Eg, f=fn, s=sem, m=n: f(Eg).then_inc(s, m))

    def raw(self, eng, fn):
        self.items[eng].append(fn)

    def emit(self, block):
        for name in self.ENGS:
            items = self.items[name]
            dec = getattr(block, name)

            def body(Eg, items=items):
                for f in items:
                    f(Eg)
            dec(body)


def ts(j, w=128):
    return slice(j * w, (j + 1) * w)


def dap(tensor, off, row_elems, free_step, free_count, parts=128):
    if isinstance(tensor, bass.AP):
        tensor = tensor.tensor
    return bass.AP(tensor, off, [[row_elems, parts], [free_step, free_count]])


class LNKit:

    def __init__(self, nc, stk, tag):
        sb = lambda n, s, d=F32: stk.enter_context(nc.sbuf_tensor(n, s, d))
        self.sq = sb(tag + "_sq", [128, E])
        self.sum = [sb(tag + "_sum%d" % i, [128, 1]) for i in range(2)]
        self.ssq = [sb(tag + "_ssq%d" % i, [128, 1]) for i in range(2)]
        self.m = [sb(tag + "_m%d" % i, [128, 1]) for i in range(2)]
        self.v = [sb(tag + "_v%d" % i, [128, 1]) for i in range(2)]
        self.d = [sb(tag + "_d%d" % i, [128, 1]) for i in range(2)]


def emit_ln(em, kit, sA, sV, slot, x_ap, out_ap):
    su, sq2, m, v, d = (kit.sum[slot], kit.ssq[slot], kit.m[slot],
                        kit.v[slot], kit.d[slot])
    em.do("scalar", lambda Eg: Eg.activation(kit.sq[:], x_ap, AF.Square,
                                             accum_out=sq2[:]))
    em.do("scalar", lambda Eg: Eg.activation(kit.sq[:], x_ap, AF.Copy,
                                             accum_out=su[:]), inc=(sA, 1))
    em.wait("vector", sA)
    em.do("vector", lambda Eg: Eg.tensor_scalar(m[:], su[:], 1.0 / E, None, ALU.mult))
    em.do("vector", lambda Eg: Eg.tensor_scalar(v[:], sq2[:], 1.0 / E, None, ALU.mult))
    em.do("vector", lambda Eg: Eg.tensor_mul(d[:], m[:], m[:]))
    em.do("vector", lambda Eg: Eg.tensor_sub(v[:], v[:], d[:]))
    em.do("vector", lambda Eg: Eg.tensor_scalar(v[:], v[:], EPS, None, ALU.add),
          inc=(sV, 1))
    em.wait("scalar", sV)
    em.do("scalar", lambda Eg: Eg.activation(d[:], v[:], AF.Ln))
    em.do("scalar", lambda Eg: Eg.activation(d[:], d[:], AF.Exp, scale=0.5),
          inc=(sA, 1))
    em.wait("vector", sA)
    em.do("vector", lambda Eg: Eg.reciprocal(d[:], d[:]))
    em.do("vector", lambda Eg: Eg.tensor_scalar(out_ap, x_ap, m[:], d[:],
                                                ALU.subtract, ALU.mult),
          inc=(sV, 1))



def build_A():
    nc = bass.Bass("TRN2", target_bir_lowering=False)
    xr = nc.dram_tensor("xr", [TPC, E], F32, kind="ExternalInput").ap()
    wqT = nc.dram_tensor("wqT", [E, E], F32, kind="ExternalInput").ap()
    wkT = nc.dram_tensor("wkT", [E, E], F32, kind="ExternalInput").ap()
    wvT = nc.dram_tensor("wvT", [E, E], F32, kind="ExternalInput").ap()
    bqv = nc.dram_tensor("bqv", [128, 8], F32, kind="ExternalInput").ap()
    bkv = nc.dram_tensor("bkv", [128, 8], F32, kind="ExternalInput").ap()
    bvv = nc.dram_tensor("bvv", [1, E], F32, kind="ExternalInput").ap()
    ropeC = nc.dram_tensor("ropeC", [128, TPC], BF16, kind="ExternalInput").ap()
    ropeS = nc.dram_tensor("ropeS", [128, TPC], BF16, kind="ExternalInput").ap()
    pswap = nc.dram_tensor("pswap", [128, 128], BF16, kind="ExternalInput").ap()
    identb = nc.dram_tensor("identb", [128, 128], BF16, kind="ExternalInput").ap()
    oqT = nc.dram_tensor("oqT", [E, TPC], BF16, kind="ExternalOutput").ap()
    okT = nc.dram_tensor("okT", [E, TPC], BF16, kind="ExternalOutput").ap()
    ov = nc.dram_tensor("ov", [TPC, E], BF16, kind="ExternalOutput").ap()

    with ExitStack() as stk:
        sb = lambda n, s, d=F32: stk.enter_context(nc.sbuf_tensor(n, s, d))
        ps = lambda n, s, d=F32: stk.enter_context(nc.psum_tensor(n, s, d))
        sem = lambda n: stk.enter_context(nc.semaphore(n))

        xr_sb = sb("xr_sb", [128, 16 * E])
        xnT = sb("xnT", [128, 8 * TPC], BF16)
        wq_sb = sb("wq_sb", [128, 8 * E], BF16)
        wk_sb = sb("wk_sb", [128, 8 * E], BF16)
        wv_sb = sb("wv_sb", [128, 8 * E], BF16)
        bq_sb = sb("bq_sb", [128, 8])
        bk_sb = sb("bk_sb", [128, 8])
        bv_sb = sb("bv_sb", [1, E])
        bvbc = sb("bvbc", [128, E])
        cosT = sb("cosT", [128, TPC], BF16)
        sinT = sb("sinT", [128, TPC], BF16)
        psw = sb("psw", [128, 128], BF16)
        idb = sb("idb", [128, 128], BF16)
        onesf = sb("onesf", [1, 128])
        lnf = sb("lnf", [128, E])
        lnb = sb("lnb", [128, E], BF16)
        qbf = [sb("qbf%d" % p, [128, 512], BF16) for p in range(2)]
        t1 = [sb("t1%d" % p, [128, 512]) for p in range(2)]
        t2 = [sb("t2%d" % p, [128, 512]) for p in range(2)]
        qout = [sb("qout%d" % p, [128, 512], BF16) for p in range(2)]
        vout = [sb("vout%d" % p, [128, 512], BF16) for p in range(2)]
        kit = LNKit(nc, stk, "ln")

        pT = [ps("pT%d" % p, [128, 128], BF16) for p in range(2)]
        pQ = [ps("pQ%d" % p, [128, 512]) for p in range(2)]
        pS2 = [ps("pS2%d" % p, [128, 512]) for p in range(2)]
        pBC = ps("pBC", [128, E])

        ld = sem("ld"); sT = sem("sT"); sV = sem("sV")
        sA = sem("sA"); sO = sem("sO")
        em = Em(nc)

        for i in range(16):
            em.do("sync", lambda Eg, i=i: Eg.dma_start(
                out=xr_sb[:, i * E:(i + 1) * E], in_=xr[ts(i), :]), inc=(ld, 16))
        em.do("sync", lambda Eg: Eg.dma_start(out=bq_sb[:, :], in_=bqv), inc=(ld, 16))
        em.do("sync", lambda Eg: Eg.dma_start(out=bk_sb[:, :], in_=bkv), inc=(ld, 16))
        em.do("sync", lambda Eg: Eg.dma_start(out=bv_sb[:, :], in_=bvv), inc=(ld, 16))
        em.do("sync", lambda Eg: Eg.dma_start(out=cosT[:], in_=ropeC), inc=(ld, 16))
        em.do("sync", lambda Eg: Eg.dma_start(out=sinT[:], in_=ropeS), inc=(ld, 16))
        em.do("sync", lambda Eg: Eg.dma_start(out=psw[:], in_=pswap), inc=(ld, 16))
        em.do("sync", lambda Eg: Eg.dma_start(out=idb[:], in_=identb), inc=(ld, 16))
        for wsb, whbm in [(wq_sb, wqT), (wk_sb, wkT), (wv_sb, wvT)]:
            for k in range(8):
                em.do("gpsimd", lambda Eg, w=wsb, h=whbm, k=k: Eg.dma_start(
                    out=w[:, k * E:(k + 1) * E], in_=h[ts(k), :]), inc=(ld, 16))
        LD_ALL = em.count(ld)

        em.wait("vector", ld, LD_ALL)
        em.do("vector", lambda Eg: Eg.memset(onesf[:], 1.0), inc=(sV, 1))

        em.wait("tensor", ld, LD_ALL)
        em.wait("tensor", sV)
        for nn in range(2):
            em.do("tensor", lambda Eg, nn=nn: Eg.matmul(
                pBC[:, nn * 512:(nn + 1) * 512], onesf[:, 0:128],
                bv_sb[:, nn * 512:(nn + 1) * 512], start=True, stop=True),
                inc=(sT, 1))
        em.wait("vector", sT)
        em.do("vector", lambda Eg: Eg.tensor_copy(bvbc[:], pBC[:]), inc=(sV, 1))

        em.wait("scalar", ld, LD_ALL)
        ptf = [None, None]
        for i in range(16):
            x_ap = xr_sb[:, i * E:(i + 1) * E]
            emit_ln(em, kit, sA, sV, i % 2, x_ap, lnf[:])
            em.do("vector", lambda Eg: Eg.tensor_copy(lnb[:], lnf[:]), inc=(sV, 1))
            em.wait("tensor", sV)
            for j in range(8):
                par = j % 2
                if ptf[par] is not None:
                    em.wait("tensor", *ptf[par])
                em.do("tensor", lambda Eg, j=j, par=par: Eg.transpose(
                    pT[par][:], lnb[:, ts(j)], idb[:]), inc=(sT, 1))
                em.wait("vector", sT)
                em.do("vector", lambda Eg, j=j, par=par, i=i: Eg.tensor_copy(
                    xnT[:, j * TPC + i * 128: j * TPC + (i + 1) * 128],
                    pT[par][:]), inc=(sV, 1))
                ptf[par] = (sV, em.count(sV))
            em.wait("vector", sT)
        XNT_DONE = em.count(sV)
        em.wait("tensor", sV, XNT_DONE)

        pq_free = [None, None]
        ps2_free = [None, None]
        qbf_free = [None, None]
        qout_free = [None, None]
        par = 0
        for (w_sb, b_sb, out_hbm, srt) in [(wq_sb, bq_sb, oqT, 1),
                                           (wk_sb, bk_sb, okT, 1)]:
            for j in range(8):
                for c in range(4):
                    if pq_free[par] is not None:
                        em.wait("tensor", *pq_free[par])
                    for k in range(8):
                        inc = (sT, 1) if k == 7 else None
                        em.do("tensor", lambda Eg, k=k, j=j, c=c, w=w_sb, par=par: Eg.matmul(
                            pQ[par][:], w[:, k * E + j * 128: k * E + (j + 1) * 128],
                            xnT[:, k * TPC + c * 512: k * TPC + (c + 1) * 512],
                            start=(k == 0), stop=(k == 7)), inc=inc)
                    em.wait("vector", sT)
                    if qbf_free[par] is not None:
                        for w8 in qbf_free[par]:
                            em.wait("vector", *w8)
                    em.do("vector", lambda Eg, j=j, b=b_sb, par=par: Eg.tensor_scalar(
                        qbf[par][:], pQ[par][:], b[:, j:j + 1], None, ALU.add),
                        inc=(sA, 1))
                    pq_free[par] = (sA, em.count(sA))
                    em.wait("tensor", sA)
                    if ps2_free[par] is not None:
                        em.wait("tensor", *ps2_free[par])
                    em.do("tensor", lambda Eg, par=par: Eg.matmul(
                        pS2[par][:], psw[:], qbf[par][:], start=True, stop=True),
                        inc=(sT, 1))
                    swap_done = em.count(sT)
                    em.wait("vector", sT, swap_done)
                    if qout_free[par] is not None:
                        em.wait("vector", *qout_free[par])
                    em.do("vector", lambda Eg, c=c, par=par: Eg.tensor_mul(
                        t1[par][:], pS2[par][:], sinT[:, c * 512:(c + 1) * 512]),
                        inc=(sV, 1))
                    ps2_free[par] = (sV, em.count(sV))
                    em.do("vector", lambda Eg, c=c, par=par: Eg.tensor_mul(
                        t2[par][:], qbf[par][:], cosT[:, c * 512:(c + 1) * 512]),
                        inc=(sV, 1))
                    qbf_free[par] = [(sT, swap_done), (sV, em.count(sV))]
                    em.do("vector", lambda Eg, par=par: Eg.tensor_add(
                        qout[par][:], t1[par][:], t2[par][:]), inc=(sV, 1))
                    em.wait("gpsimd", sV)
                    em.do("gpsimd", lambda Eg, j=j, c=c, o=out_hbm, par=par: Eg.dma_start(
                        out=o[ts(j), c * 512:(c + 1) * 512], in_=qout[par][:]),
                        inc=(sO, 16))
                    qout_free[par] = (sO, em.count(sO))
                    par ^= 1

        vout_free = [None, None]
        for i in range(16):
            for nn in range(2):
                if pq_free[par] is not None:
                    em.wait("tensor", *pq_free[par])
                for k in range(8):
                    inc = (sT, 1) if k == 7 else None
                    em.do("tensor", lambda Eg, k=k, i=i, nn=nn, par=par: Eg.matmul(
                        pQ[par][:], xnT[:, k * TPC + i * 128: k * TPC + (i + 1) * 128],
                        wv_sb[:, k * E + nn * 512: k * E + (nn + 1) * 512],
                        start=(k == 0), stop=(k == 7)), inc=inc)
                em.wait("vector", sT)
                if vout_free[par] is not None:
                    em.wait("vector", *vout_free[par])
                em.do("vector", lambda Eg, nn=nn, par=par: Eg.tensor_add(
                    vout[par][:], pQ[par][:], bvbc[:, nn * 512:(nn + 1) * 512]),
                    inc=(sV, 1))
                pq_free[par] = (sV, em.count(sV))
                em.wait("gpsimd", sV)
                em.do("gpsimd", lambda Eg, i=i, nn=nn, par=par: Eg.dma_start(
                    out=ov[ts(i), nn * 512:(nn + 1) * 512], in_=vout[par][:]),
                    inc=(sO, 16))
                vout_free[par] = (sO, em.count(sO))
                par ^= 1

        em.wait("gpsimd", sO, em.count(sO))
        with nc.Block() as block:
            em.emit(block)
    return nc



def build_B():
    nc = bass.Bass("TRN2", target_bir_lowering=False)
    qTh = nc.dram_tensor("qTh", [512, S], BF16, kind="ExternalInput").ap()
    kTh = nc.dram_tensor("kTh", [512, S], BF16, kind="ExternalInput").ap()
    vh = nc.dram_tensor("vh", [S, 512], BF16, kind="ExternalInput").ap()
    oTh = nc.dram_tensor("oTh", [512, S], BF16, kind="ExternalOutput").ap()

    with ExitStack() as stk:
        sb = lambda n, s, d=F32: stk.enter_context(nc.sbuf_tensor(n, s, d))
        ps = lambda n, s, d=F32: stk.enter_context(nc.psum_tensor(n, s, d))
        sem = lambda n: stk.enter_context(nc.semaphore(n))

        qt_sb = sb("qt_sb", [128, 4 * S], BF16)
        kt_sb = sb("kt_sb", [128, 4 * S], BF16)
        vv_sb = sb("vv_sb", [128, 32 * 512], BF16)
        vx = sb("vx", [128, 32 * 65], BF16)
        P = [sb("P%d" % p, [128, 1024], BF16) for p in range(2)]
        onesf = sb("onesf", [1, 64])
        r1 = sb("r1", [1, 512])
        o_f = sb("o_f", [64, 512])
        obf = [sb("obf%d" % p, [64, 512], BF16) for p in range(2)]

        pS = [ps("pS%d" % p, [128, 1024]) for p in range(2)]
        pO = ps("pO", [65, 512])
        pR = ps("pR", [64, 512])

        ld = sem("ld"); sT = sem("sT"); sV = sem("sV")
        sA = sem("sA"); sO = sem("sO")
        em = Em(nc)

        for t in range(4):
            em.do("sync", lambda Eg, t=t: Eg.dma_start(
                out=qt_sb[:, t * S:(t + 1) * S], in_=qTh[ts(t), :]), inc=(ld, 16))
            em.do("sync", lambda Eg, t=t: Eg.dma_start(
                out=kt_sb[:, t * S:(t + 1) * S], in_=kTh[ts(t), :]), inc=(ld, 16))
        for kk in range(32):
            em.do("sync", lambda Eg, kk=kk: Eg.dma_start(
                out=vv_sb[:, kk * 512:(kk + 1) * 512], in_=vh[ts(kk), :]), inc=(ld, 16))
        LD_ALL = em.count(ld)

        em.wait("vector", ld, LD_ALL)
        em.do("vector", lambda Eg: Eg.memset(onesf[:], 1.0), inc=(sV, 1))
        em.wait("scalar", ld, LD_ALL)
        em.wait("tensor", ld, LD_ALL)

        psfree = [None, None]
        pOfree = None
        pRfree = None
        obfree = [None, None]
        pp = 0
        for h in range(8):
            tq = h // 2
            base = 64 * (h % 2)
            vx_wait = em.count(sT)
            em.wait("vector", sT, vx_wait)
            em.do("vector", lambda Eg: Eg.memset(vx[:], 1.0))
            for kk in range(32):
                em.do("vector", lambda Eg, kk=kk, h=h: Eg.tensor_copy(
                    vx[:, kk * 65: kk * 65 + 64],
                    vv_sb[:, kk * 512 + 64 * h: kk * 512 + 64 * h + 64]),
                    inc=((sV, 1) if kk == 31 else None))
            em.wait("tensor", sV)
            em.wait("scalar", sV)
            for qc in range(8):
                avq = []
                for kp in range(16):
                    par = kp % 2
                    if psfree[par] is not None:
                        em.wait("tensor", *psfree[par])
                    for half in range(2):
                        kk = 2 * kp + half
                        inc = (sT, 1) if half == 1 else None
                        em.do("tensor", lambda Eg, kk=kk, half=half, par=par,
                              tq=tq, base=base, qc=qc: Eg.matmul(
                                  pS[par][:, half * 512:(half + 1) * 512],
                                  kt_sb[base:base + 64,
                                        tq * S + kk * 128: tq * S + (kk + 1) * 128],
                                  qt_sb[base:base + 64,
                                        tq * S + qc * 512: tq * S + (qc + 1) * 512],
                                  start=True, stop=True), inc=inc)
                    sdone = em.count(sT)
                    em.wait("scalar", sT, sdone)
                    em.do("scalar", lambda Eg, par=par: Eg.activation(
                        P[par][:], pS[par][:], AF.Exp, scale=0.125), inc=(sA, 1))
                    psfree[par] = (sA, em.count(sA))
                    avq.append((par, em.count(sA), kp))
                    if len(avq) >= 2:
                        em_av(em, sT, sA, sV, pO, vx, P, avq.pop(0), pOfree, 16)
                        pOfree = None
                last = avq.pop(0)
                em_av(em, sT, sA, sV, pO, vx, P, last, pOfree, 16)
                pOfree = None
                avdone = em.count(sT)
                em.wait("vector", sT, avdone)
                em.do("vector", lambda Eg: Eg.reciprocal(r1[:], pO[64:65, :]),
                      inc=(sV, 1))
                em.wait("tensor", sV)
                if pRfree is not None:
                    em.wait("tensor", *pRfree)
                em.do("tensor", lambda Eg: Eg.matmul(
                    pR[:], onesf[:, :], r1[:], start=True, stop=True), inc=(sT, 1))
                em.wait("vector", sT)
                em.do("vector", lambda Eg: Eg.tensor_copy(o_f[:], pO[0:64, :]),
                      inc=(sV, 1))
                pOfree = (sV, em.count(sV))
                if obfree[pp] is not None:
                    em.wait("vector", *obfree[pp])
                em.do("vector", lambda Eg, pp=pp: Eg.tensor_mul(
                    obf[pp][:], o_f[:], pR[:, :]), inc=(sV, 1))
                pRfree = (sV, em.count(sV))
                em.wait("gpsimd", sV)
                em.do("gpsimd", lambda Eg, h=h, qc=qc, pp=pp: Eg.dma_start(
                    out=oTh[64 * h:64 * h + 64, qc * 512:(qc + 1) * 512],
                    in_=obf[pp][:]), inc=(sO, 16))
                obfree[pp] = (sO, em.count(sO))
                pp ^= 1

        em.wait("gpsimd", sO, em.count(sO))
        with nc.Block() as block:
            em.emit(block)
    return nc


def em_av(em, sT, sA, sV, pO, vx, P, item, pOfree, nkp):
    par, expdone, kp = item
    em.wait("tensor", sA, expdone)
    if kp == 0 and pOfree is not None:
        em.wait("tensor", *pOfree)
    for half in range(2):
        kk = 2 * kp + half
        em.do("tensor", lambda Eg, kk=kk, half=half, par=par, kp=kp: Eg.matmul(
            pO[:], vx[:, kk * 65:(kk + 1) * 65],
            P[par][:, half * 512:(half + 1) * 512],
            start=(kp == 0 and half == 0), stop=(kp == nkp - 1 and half == 1)),
            inc=((sT, 1) if half == 1 else None))



def build_CD():
    nc = bass.Bass("TRN2", target_bir_lowering=False)
    otc = nc.dram_tensor("otc", [E, TPC], BF16, kind="ExternalInput").ap()
    xrc = nc.dram_tensor("xrc", [TPC, E], F32, kind="ExternalInput").ap()
    woT = nc.dram_tensor("woT", [E, E], F32, kind="ExternalInput").ap()
    bo = nc.dram_tensor("bo", [1, E], F32, kind="ExternalInput").ap()
    w0T = nc.dram_tensor("w0T", [E, E], F32, kind="ExternalInput").ap()
    wi1T = nc.dram_tensor("wi1T", [E, E], F32, kind="ExternalInput").ap()
    wif1T = nc.dram_tensor("wif1T", [E, E], F32, kind="ExternalInput").ap()
    bhf1 = nc.dram_tensor("bhf1", [128, 8], F32, kind="ExternalInput").ap()
    gamv = nc.dram_tensor("gamv", [128, 8], F32, kind="ExternalInput").ap()
    identb = nc.dram_tensor("identb", [128, 128], BF16, kind="ExternalInput").ap()
    identf = nc.dram_tensor("identf", [128, 128], F32, kind="ExternalInput").ap()
    x2r = nc.dram_tensor("x2r", [TPC, E], F32, kind="ExternalOutput").ap()
    zaT = nc.dram_tensor("zaT", [E, TPC], F32, kind="ExternalOutput").ap()
    xnTf = nc.dram_tensor("xnTf", [E, TPC], F32, kind="ExternalOutput").ap()
    xnTb = nc.dram_tensor("xnTb", [E, TPC], BF16, kind="ExternalOutput").ap()
    u1c = nc.dram_tensor("u1c", [128, SPC * 32], F32, kind="ExternalOutput").ap()
    c1c = nc.dram_tensor("c1c", [128, SPC * 32], F32, kind="ExternalOutput").ap()

    with ExitStack() as stk:
        sb = lambda n, s, d=F32: stk.enter_context(nc.sbuf_tensor(n, s, d))
        ps = lambda n, s, d=F32: stk.enter_context(nc.psum_tensor(n, s, d))
        sem = lambda n: stk.enter_context(nc.semaphore(n))

        wo_sb = sb("wo_sb", [128, 8 * E], BF16)
        sw_sb = sb("sw_sb", [128, 8 * E], BF16)
        wi1_sb = sb("wi1_sb", [128, 8 * E], BF16)
        wif1_sb = sb("wif1_sb", [128, 8 * E], BF16)
        bhf_sb = sb("bhf_sb", [128, 8])
        gam_sb = sb("gam_sb", [128, 8])
        gp_sb = sb("gp_sb", [128, 8])
        idb = sb("idb", [128, 128], BF16)
        idf = sb("idf", [128, 128])
        onesf = sb("onesf", [1, 128])
        bo_sb = sb("bo_sb", [1, E])
        bobc = sb("bobc", [128, E])
        ot_sb = sb("ot_sb", [128, 8 * 512], BF16)
        xa = sb("xa", [128, 4 * E])
        xtile = sb("xtile", [128, E])
        f0tile = sb("f0tile", [128, E])
        zatile = sb("zatile", [128, E])
        y1tile = sb("y1tile", [128, E])
        lnbt = sb("lnbt", [128, E], BF16)
        ln2T = sb("ln2T", [128, 8 * 512], BF16)
        zaT_sb = sb("zaT_sb", [128, 8 * 512])
        xnrT = sb("xnrT", [128, 8 * 512])
        xnTb_sb = sb("xnTb_sb", [128, 8 * 512], BF16)
        tf = sb("tf", [128, 512])
        tg = sb("tg", [128, 512])
        ustage = sb("ustage", [128, CH * 32])
        cstage = sb("cstage", [128, CH * 32])
        w0_sb = ustage[:].bitcast(BF16)
        kit = LNKit(nc, stk, "ln")
        rs_sq = sb("rs_sq", [128, E])
        rs_s = sb("rs_s", [128, 1])
        rs_d = sb("rs_d", [128, 1])

        pY = [ps("pY%d" % p, [128, 512]) for p in range(2)]
        pTb = [ps("pTb%d" % p, [128, 128], BF16) for p in range(2)]
        pTf = [ps("pTf%d" % p, [128, 128]) for p in range(2)]
        pBC = ps("pBC", [128, E])

        ld = sem("ld"); sT = sem("sT"); sV = sem("sV")
        sA = sem("sA"); sO = sem("sO")
        em = Em(nc)

        for wsb, whbm in [(wo_sb, woT), (w0_sb, w0T), (wi1_sb, wi1T),
                          (wif1_sb, wif1T)]:
            for k in range(8):
                em.do("gpsimd", lambda Eg, w=wsb, h=whbm, k=k: Eg.dma_start(
                    out=w[:, k * E:(k + 1) * E], in_=h[ts(k), :]), inc=(ld, 16))
        em.do("sync", lambda Eg: Eg.dma_start(out=bhf_sb[:], in_=bhf1), inc=(ld, 16))
        em.do("sync", lambda Eg: Eg.dma_start(out=gam_sb[:], in_=gamv), inc=(ld, 16))
        em.do("sync", lambda Eg: Eg.dma_start(out=idb[:], in_=identb), inc=(ld, 16))
        em.do("sync", lambda Eg: Eg.dma_start(out=idf[:], in_=identf), inc=(ld, 16))
        em.do("sync", lambda Eg: Eg.dma_start(out=bo_sb[:], in_=bo), inc=(ld, 16))
        LD_W = em.count(ld)

        em.wait("scalar", ld, LD_W)
        em.do("scalar", lambda Eg: Eg.activation(gp_sb[:], gam_sb[:], AF.Copy,
                                                 bias=32.0, scale=32.0))
        for k in range(8):
            em.do("scalar", lambda Eg, k=k: Eg.activation(
                sw_sb[:, k * E:(k + 1) * E], w0_sb[:, k * E:(k + 1) * E],
                AF.Sign), inc=((sA, 1) if k == 7 else None))
        em.wait("vector", ld, LD_W)
        em.do("vector", lambda Eg: Eg.memset(onesf[:], 1.0), inc=(sV, 1))
        em.wait("tensor", sV)
        em.wait("tensor", ld, LD_W)
        for nn in range(2):
            em.do("tensor", lambda Eg, nn=nn: Eg.matmul(
                pBC[:, nn * 512:(nn + 1) * 512], onesf[:, 0:128],
                bo_sb[:, nn * 512:(nn + 1) * 512], start=True, stop=True),
                inc=(sT, 1))
        em.wait("vector", sT)
        em.do("vector", lambda Eg: Eg.tensor_copy(bobc[:], pBC[:]), inc=(sV, 1))

        py_free = [None, None]
        ptb_free = [None, None]
        ptf_free = [None, None]
        stage_free = None
        par = 0
        tpar = 0

        def mm8(lhs_fn, rhs_fn, wait_extra=None):
            nonlocal par
            if py_free[par] is not None:
                em.wait("tensor", *py_free[par])
            if wait_extra is not None:
                em.wait("tensor", *wait_extra)
            for k in range(8):
                em.do("tensor", lambda Eg, k=k, lf=lhs_fn, rf=rhs_fn, par=par:
                      Eg.matmul(pY[par][:], lf(k), rf(k),
                                start=(k == 0), stop=(k == 7)),
                      inc=((sT, 1) if k == 7 else None))
            return em.count(sT)

        def transp(src_ap_fn, dst_fn, n, fp32, done_engine="vector"):
            nonlocal tpar
            pT_ = pTf if fp32 else pTb
            ptfree = ptf_free if fp32 else ptb_free
            ident = idf if fp32 else idb
            for j in range(n):
                q = tpar
                if ptfree[q] is not None:
                    em.wait("tensor", *ptfree[q])
                em.do("tensor", lambda Eg, j=j, q=q, ident=ident, pT_=pT_:
                      Eg.transpose(pT_[q][:], src_ap_fn(j), ident[:]),
                      inc=(sT, 1))
                em.wait("vector", sT)
                em.do("vector", lambda Eg, j=j, q=q, pT_=pT_: Eg.tensor_copy(
                    dst_fn(j), pT_[q][:]), inc=(sV, 1))
                ptfree[q] = (sV, em.count(sV))
                tpar ^= 1

        for cc in range(4):
            tok0 = cc * 512
            em.wait("sync", sT, em.count(sT))
            for j in range(8):
                em.do("sync", lambda Eg, j=j, cc=cc: Eg.dma_start(
                    out=ot_sb[:, j * 512:(j + 1) * 512],
                    in_=otc[ts(j), cc * 512:(cc + 1) * 512]), inc=(ld, 16))
            OT_LD = em.count(ld)
            em.wait("vector", sO)
            for tt in range(4):
                em.wait("tensor", ld, OT_LD)
                for nn in range(2):
                    ms = mm8(lambda k, tt=tt: ot_sb[:, k * 512 + tt * 128:
                                                    k * 512 + (tt + 1) * 128],
                             lambda k, nn=nn: wo_sb[:, k * E + nn * 512:
                                                    k * E + (nn + 1) * 512])
                    em.wait("vector", sT, ms)
                    em.do("vector", lambda Eg, tt=tt, nn=nn, par=par: Eg.tensor_add(
                        xa[:, tt * E + nn * 512: tt * E + (nn + 1) * 512],
                        pY[par][:], bobc[:, nn * 512:(nn + 1) * 512]),
                        inc=(sV, 1))
                    py_free[par] = (sV, em.count(sV))
                    par ^= 1
                xa_t = xa[:, tt * E:(tt + 1) * E]
                em.wait("sync", sV)
                em.do("sync", lambda Eg, cc=cc, tt=tt: Eg.dma_start(
                    out=xtile[:], in_=xrc[ts(cc * 4 + tt), :]), inc=(ld, 16))
                XT_LD = em.count(ld)
                em.wait("scalar", sV)
                em.wait("vector", sT)
                emit_ln(em, kit, sA, sV, 0, xa_t, zatile[:])
                em.wait("vector", ld, XT_LD)
                em.do("vector", lambda Eg, xa_t=xa_t: Eg.tensor_add(
                    xa_t, zatile[:], xtile[:]), inc=(sV, 1))
                em.wait("scalar", sV)
                em.wait("vector", sT)
                emit_ln(em, kit, sA, sV, 0, xa_t, zatile[:])
                em.do("vector", lambda Eg: Eg.tensor_copy(lnbt[:], zatile[:]),
                      inc=(sV, 1))
                em.wait("tensor", sV)
                transp(lambda j: lnbt[:, ts(j)],
                       lambda j, tt=tt: ln2T[:, j * 512 + tt * 128:
                                             j * 512 + (tt + 1) * 128],
                       8, fp32=False)
                em.wait("tensor", sA, 1)
                em.wait("tensor", sV)
                for nn in range(2):
                    ms = mm8(lambda k, tt=tt: ln2T[:, k * 512 + tt * 128:
                                                   k * 512 + (tt + 1) * 128],
                             lambda k, nn=nn: sw_sb[:, k * E + nn * 512:
                                                    k * E + (nn + 1) * 512])
                    em.wait("vector", sT, ms)
                    em.do("vector", lambda Eg, nn=nn, par=par: Eg.tensor_copy(
                        f0tile[:, nn * 512:(nn + 1) * 512], pY[par][:]),
                        inc=(sV, 1))
                    py_free[par] = (sV, em.count(sV))
                    par ^= 1
                em.wait("scalar", sV)
                em.wait("vector", sT)
                emit_ln(em, kit, sA, sV, 0, f0tile[:], zatile[:])
                em.do("vector", lambda Eg, xa_t=xa_t: Eg.tensor_add(
                    xa_t, xa_t, zatile[:]), inc=(sV, 1))
                X2_DONE = em.count(sV)
                em.wait("gpsimd", sV, X2_DONE)
                em.do("gpsimd", lambda Eg, cc=cc, tt=tt, xa_t=xa_t: Eg.dma_start(
                    out=x2r[ts(cc * 4 + tt), :], in_=xa_t), inc=(sO, 16))
                em.wait("scalar", sV, X2_DONE)
                em.wait("vector", sT)
                emit_ln(em, kit, sA, sV, 0, xa_t, zatile[:])
                em.wait("tensor", sV)
                transp(lambda j: zatile[:, ts(j)],
                       lambda j, tt=tt: zaT_sb[:, j * 512 + tt * 128:
                                               j * 512 + (tt + 1) * 128],
                       8, fp32=True)
                em.wait("scalar", sV)
                em.do("scalar", lambda Eg: Eg.activation(
                    rs_sq[:], zatile[:], AF.Square, accum_out=rs_s[:]))
                em.do("scalar", lambda Eg: Eg.activation(rs_d[:], rs_s[:], AF.Ln))
                em.do("scalar", lambda Eg: Eg.activation(
                    rs_d[:], rs_d[:], AF.Exp, scale=0.5), inc=(sA, 1))
                em.wait("vector", sA)
                em.do("vector", lambda Eg: Eg.tensor_scalar_max(
                    rs_d[:], rs_d[:], 1e-12))
                em.do("vector", lambda Eg: Eg.reciprocal(rs_d[:], rs_d[:]))
                em.wait("vector", sT)
                em.do("vector", lambda Eg: Eg.tensor_scalar(
                    y1tile[:], zatile[:], rs_d[:], None, ALU.mult),
                    inc=(sV, 1))
                em.wait("tensor", sV)
                transp(lambda j: y1tile[:, ts(j)],
                       lambda j, tt=tt: xnrT[:, j * 512 + tt * 128:
                                             j * 512 + (tt + 1) * 128],
                       8, fp32=True)
            em.wait("vector", sT)
            for j in range(8):
                em.do("vector", lambda Eg, j=j: Eg.tensor_scalar(
                    xnrT[:, j * 512:(j + 1) * 512],
                    xnrT[:, j * 512:(j + 1) * 512],
                    gp_sb[:, j:j + 1], None, ALU.mult))
                em.do("vector", lambda Eg, j=j: Eg.tensor_copy(
                    xnTb_sb[:, j * 512:(j + 1) * 512],
                    xnrT[:, j * 512:(j + 1) * 512]), inc=(sV, 1))
            XN_DONE = em.count(sV)
            em.wait("gpsimd", sV, XN_DONE)
            for j in range(8):
                em.do("gpsimd", lambda Eg, j=j, cc=cc: Eg.dma_start(
                    out=zaT[ts(j), cc * 512:(cc + 1) * 512],
                    in_=zaT_sb[:, j * 512:(j + 1) * 512]), inc=(sO, 16))
                em.do("gpsimd", lambda Eg, j=j, cc=cc: Eg.dma_start(
                    out=xnTf[ts(j), cc * 512:(cc + 1) * 512],
                    in_=xnrT[:, j * 512:(j + 1) * 512]), inc=(sO, 16))
                em.do("gpsimd", lambda Eg, j=j, cc=cc: Eg.dma_start(
                    out=xnTb[ts(j), cc * 512:(cc + 1) * 512],
                    in_=xnTb_sb[:, j * 512:(j + 1) * 512]), inc=(sO, 16))
            if stage_free is not None:
                em.wait("vector", *stage_free)
            em.wait("tensor", sV, XN_DONE)
            for j in range(8):
                ms = mm8(lambda k, j=j: wi1_sb[:, k * E + j * 128:
                                               k * E + (j + 1) * 128],
                         lambda k: xnTb_sb[:, k * 512:(k + 1) * 512])
                em.wait("scalar", sT, ms)
                em.wait("scalar", sV)
                em.do("scalar", lambda Eg, par=par: Eg.activation(
                    tf[:], pY[par][:], AF.Tanh))
                em.do("scalar", lambda Eg: Eg.activation(tg[:], tf[:], AF.Tanh),
                      inc=(sA, 1))
                py_free[par] = (sA, em.count(sA))
                par ^= 1
                em.wait("vector", sA)
                for b in range(4):
                    em.do("vector", lambda Eg, j=j, b=b: Eg.tensor_copy(
                        dap(ustage, 4 * j + b, CH * 32, 32, CH),
                        dap(tg, b, 512, 4, CH)),
                        inc=((sV, 1) if b == 3 else None))
                ms = mm8(lambda k, j=j: wif1_sb[:, k * E + j * 128:
                                                k * E + (j + 1) * 128],
                         lambda k: xnTb_sb[:, k * 512:(k + 1) * 512])
                em.wait("vector", sT, ms)
                em.do("vector", lambda Eg, j=j, par=par: Eg.tensor_scalar(
                    tf[:], pY[par][:], bhf_sb[:, j:j + 1], None, ALU.add),
                    inc=(sA, 1))
                py_free[par] = (sA, em.count(sA))
                par ^= 1
                em.wait("vector", sA)
                for b in range(4):
                    em.do("vector", lambda Eg, j=j, b=b: Eg.tensor_copy(
                        dap(cstage, 4 * j + b, CH * 32, 32, CH),
                        dap(tf, b, 512, 4, CH)),
                        inc=((sV, 1) if b == 3 else None))
            PACK_DONE = em.count(sV)
            em.wait("gpsimd", sV, PACK_DONE)
            em.do("gpsimd", lambda Eg, cc=cc: Eg.dma_start(
                out=u1c[:, cc * CH * 32:(cc + 1) * CH * 32], in_=ustage[:]),
                inc=(sO, 16))
            em.do("gpsimd", lambda Eg, cc=cc: Eg.dma_start(
                out=c1c[:, cc * CH * 32:(cc + 1) * CH * 32], in_=cstage[:]),
                inc=(sO, 16))
            stage_free = (sO, em.count(sO))

        em.wait("gpsimd", sO, em.count(sO))
        with nc.Block() as block:
            em.emit(block)
    return nc



def build_scan():
    nc = bass.Bass("TRN2", target_bir_lowering=False)
    U = nc.dram_tensor("U", [128, 32 * S], F32, kind="ExternalInput").ap()
    C = nc.dram_tensor("C", [128, 32 * S], F32, kind="ExternalInput").ap()
    whfT = nc.dram_tensor("whfT", [E, E], F32, kind="ExternalInput").ap()
    h0p = nc.dram_tensor("h0p", [128, 32], F32, kind="ExternalInput").ap()
    Hs = nc.dram_tensor("Hs", [128, 32 * S], BF16, kind="ExternalOutput").ap()
    BUF = 2 * CH * 32

    with ExitStack() as stk:
        sb = lambda n, s, d=F32: stk.enter_context(nc.sbuf_tensor(n, s, d))
        ps = lambda n, s, d=F32: stk.enter_context(nc.psum_tensor(n, s, d))
        sem = lambda n: stk.enter_context(nc.semaphore(n))

        wsc = sb("wsc", [128, 8 * E], BF16)
        ubuf = sb("ubuf", [128, BUF])
        cbuf = sb("cbuf", [128, BUF])
        hsb = sb("hsb", [128, BUF], BF16)
        hTf = sb("hTf", [128, 32])
        hTb = sb("hTb", [128, 32], BF16)
        sgt = sb("sgt", [128, 32])
        gt = sb("gt", [128, 32])
        dt_ = sb("dt_", [128, 32])
        pA = ps("pA", [128, 32])

        ld = sem("ld")
        sT = sem("sT")
        sV = sem("sV")
        sVc = sem("sVc")
        sAx = sem("sAx")
        sO = sem("sO")

        with nc.Block() as blk:
            @blk.gpsimd
            def _(Eg):
                for k in range(8):
                    Eg.dma_start(out=wsc[:, k * E:(k + 1) * E],
                                 in_=whfT[ts(k), :]).then_inc(ld, 16)
                Eg.sem_inc(sO, 16)

            @blk.sync
            def _(Eg):
                Eg.dma_start(out=hTf[:], in_=h0p).then_inc(ld, 16)
                Eg.dma_start(out=ubuf[:, 0:CH * 32],
                             in_=U[:, 0:CH * 32]).then_inc(ld, 16)
                Eg.dma_start(out=cbuf[:, 0:CH * 32],
                             in_=C[:, 0:CH * 32]).then_inc(ld, 16)

            @blk.vector
            def _(Eg):
                Eg.wait_ge(ld, 16 * 11)
                Eg.tensor_copy(hTb[:], hTf[:]).then_inc(sV, 1)

        with nc.tensor.Fori(0, S) as i:
            tv = nc.tensor
            tv.wait_ge(sV, i + 1)
            for m in range(8):
                for k in range(8):
                    ii = tv.matmul(
                        pA[:, 4 * m:4 * m + 4],
                        wsc[:, k * E + m * 128: k * E + (m + 1) * 128],
                        hTb[:, 4 * k:4 * k + 4],
                        start=(k == 0), stop=(k == 7))
                    if m == 7 and k == 7:
                        ii.then_inc(sT, 1)
        with nc.scalar.Fori(0, S) as i:
            av = nc.scalar
            av.wait_ge(sVc, i + 1)
            av.activation(gt[:], sgt[:], AF.Sigmoid).then_inc(sAx, 1)
        with nc.vector.Fori(0, S) as i:
            vv = nc.vector
            ph = (i % (2 * CH)) * 32
            chv = i // CH
            vv.wait_ge(ld, 16 * 9 + chv * 32 + 32)
            vv.wait_ge(sO, chv * 16)
            vv.tensor_sub(dt_[:], dap(ubuf, ph, BUF, 1, 32), hTf[:])
            vv.wait_ge(sT, i + 1)
            vv.tensor_add(sgt[:], pA[:], dap(cbuf, ph, BUF, 1, 32)
                          ).then_inc(sVc, 1)
            vv.wait_ge(sAx, i + 1)
            vv.tensor_mul(dt_[:], gt[:], dt_[:])
            vv.tensor_add(hTf[:], hTf[:], dt_[:])
            vv.tensor_copy(dap(hsb, ph, BUF, 1, 32), hTf[:])
            vv.tensor_copy(hTb[:], hTf[:]).then_inc(sV, 1)

        with nc.Block() as blk:
            @blk.gpsimd
            def _(Eg):
                for k in range(NCHUNK):
                    if k + 1 < NCHUNK:
                        Eg.wait_ge(sV, k * CH + 1)
                        off = ((k + 1) % 2) * (CH * 32)
                        h0_ = (k + 1) * (CH * 32)
                        Eg.dma_start(out=ubuf[:, off:off + CH * 32],
                                     in_=U[:, h0_:h0_ + CH * 32]).then_inc(ld, 16)
                        Eg.dma_start(out=cbuf[:, off:off + CH * 32],
                                     in_=C[:, h0_:h0_ + CH * 32]).then_inc(ld, 16)
                    Eg.wait_ge(sV, (k + 1) * CH + 1)
                    goff = (k % 2) * (CH * 32)
                    gh = k * (CH * 32)
                    Eg.dma_start(out=Hs[:, gh:gh + CH * 32],
                                 in_=hsb[:, goff:goff + CH * 32]).then_inc(sO, 16)

        with nc.Block() as blk:
            @blk.gpsimd
            def _(Eg):
                Eg.wait_ge(sO, 16 * (NCHUNK + 1))
    return nc



def build_F():
    nc = bass.Bass("TRN2", target_bir_lowering=False)
    h1c = nc.dram_tensor("h1c", [128, SPC * 32], BF16, kind="ExternalInput").ap()
    wi2T = nc.dram_tensor("wi2T", [E, E], F32, kind="ExternalInput").ap()
    wif2T = nc.dram_tensor("wif2T", [E, E], F32, kind="ExternalInput").ap()
    bhf2 = nc.dram_tensor("bhf2", [128, 8], F32, kind="ExternalInput").ap()
    u2c = nc.dram_tensor("u2c", [128, SPC * 32], F32, kind="ExternalOutput").ap()
    c2c = nc.dram_tensor("c2c", [128, SPC * 32], F32, kind="ExternalOutput").ap()

    with ExitStack() as stk:
        sb = lambda n, s, d=F32: stk.enter_context(nc.sbuf_tensor(n, s, d))
        ps = lambda n, s, d=F32: stk.enter_context(nc.psum_tensor(n, s, d))
        sem = lambda n: stk.enter_context(nc.semaphore(n))

        slab = sb("slab", [128, SPC * 32], BF16)
        h1T = sb("h1T", [128, 8 * TPC], BF16)
        wi2_sb = sb("wi2_sb", [128, 8 * E], BF16)
        wif2_sb = sb("wif2_sb", [128, 8 * E], BF16)
        bhf_sb = sb("bhf_sb", [128, 8])
        tf = sb("tf", [128, 512])
        tg = sb("tg", [128, 512])
        ustage = sb("ustage", [128, CH * 32])
        cstage = sb("cstage", [128, CH * 32])
        pY = [ps("pY%d" % p, [128, 512]) for p in range(2)]

        ld = sem("ld"); sT = sem("sT"); sV = sem("sV")
        sA = sem("sA"); sO = sem("sO")
        em = Em(nc)

        em.do("sync", lambda Eg: Eg.dma_start(out=slab[:], in_=h1c), inc=(ld, 16))
        em.do("sync", lambda Eg: Eg.dma_start(out=bhf_sb[:], in_=bhf2), inc=(ld, 16))
        for wsb, whbm in [(wi2_sb, wi2T), (wif2_sb, wif2T)]:
            for k in range(8):
                em.do("gpsimd", lambda Eg, w=wsb, h=whbm, k=k: Eg.dma_start(
                    out=w[:, k * E:(k + 1) * E], in_=h[ts(k), :]), inc=(ld, 16))
        LD_ALL = em.count(ld)
        em.wait("vector", ld, LD_ALL)
        for j in range(8):
            for b in range(4):
                em.do("vector", lambda Eg, j=j, b=b: Eg.tensor_copy(
                    dap(h1T, j * TPC + b, TPC * 8, 4, SPC),
                    dap(slab, 4 * j + b, SPC * 32, 32, SPC)),
                    inc=((sV, 1) if (j, b) == (7, 3) else None))
        em.wait("tensor", sV)
        em.wait("tensor", ld, LD_ALL)
        em.wait("scalar", ld, LD_ALL)

        py_free = [None, None]
        stage_free = None
        par = 0
        for cc in range(4):
            for j in range(8):
                for (w_sb, is_u) in [(wi2_sb, True), (wif2_sb, False)]:
                    if py_free[par] is not None:
                        em.wait("tensor", *py_free[par])
                    for k in range(8):
                        em.do("tensor", lambda Eg, k=k, j=j, cc=cc, w=w_sb, par=par:
                              Eg.matmul(pY[par][:],
                                        w[:, k * E + j * 128: k * E + (j + 1) * 128],
                                        h1T[:, k * TPC + cc * 512: k * TPC + cc * 512 + 512],
                                        start=(k == 0), stop=(k == 7)),
                              inc=((sT, 1) if k == 7 else None))
                    em.wait("scalar", sT)
                    em.wait("scalar", sV)
                    if is_u:
                        em.do("scalar", lambda Eg, par=par: Eg.activation(
                            tf[:], pY[par][:], AF.Tanh))
                        em.do("scalar", lambda Eg: Eg.activation(
                            tg[:], tf[:], AF.Tanh), inc=(sA, 1))
                    else:
                        em.wait("vector", sT)
                        em.do("vector", lambda Eg, j=j, par=par: Eg.tensor_scalar(
                            tg[:], pY[par][:], bhf_sb[:, j:j + 1], None, ALU.add),
                            inc=(sA, 1))
                    py_free[par] = (sA, em.count(sA))
                    par ^= 1
                    em.wait("vector", sA)
                    if stage_free is not None:
                        em.wait("vector", *stage_free)
                        stage_free = None
                    stg = ustage if is_u else cstage
                    for b in range(4):
                        em.do("vector", lambda Eg, j=j, b=b, stg=stg: Eg.tensor_copy(
                            dap(stg, 4 * j + b, CH * 32, 32, CH),
                            dap(tg, b, 512, 4, CH)),
                            inc=((sV, 1) if b == 3 else None))
            PACK = em.count(sV)
            em.wait("gpsimd", sV, PACK)
            em.do("gpsimd", lambda Eg, cc=cc: Eg.dma_start(
                out=u2c[:, cc * CH * 32:(cc + 1) * CH * 32], in_=ustage[:]),
                inc=(sO, 16))
            em.do("gpsimd", lambda Eg, cc=cc: Eg.dma_start(
                out=c2c[:, cc * CH * 32:(cc + 1) * CH * 32], in_=cstage[:]),
                inc=(sO, 16))
            stage_free = (sO, em.count(sO))

        em.wait("gpsimd", sO, em.count(sO))
        with nc.Block() as block:
            em.emit(block)
    return nc



def build_H():
    nc = bass.Bass("TRN2", target_bir_lowering=False)
    h2c = nc.dram_tensor("h2c", [128, SPC * 32], BF16, kind="ExternalInput").ap()
    xnTbi = nc.dram_tensor("xnTbi", [E, TPC], BF16, kind="ExternalInput").ap()
    xnTfi = nc.dram_tensor("xnTfi", [E, TPC], F32, kind="ExternalInput").ap()
    zaTi = nc.dram_tensor("zaTi", [E, TPC], F32, kind="ExternalInput").ap()
    x2rc = nc.dram_tensor("x2rc", [TPC, E], F32, kind="ExternalInput").ap()
    gWiT = nc.dram_tensor("gWiT", [E, E], F32, kind="ExternalInput").ap()
    gWifT = nc.dram_tensor("gWifT", [E, E], F32, kind="ExternalInput").ap()
    gWhfT = nc.dram_tensor("gWhfT", [E, E], F32, kind="ExternalInput").ap()
    gbhf = nc.dram_tensor("gbhf", [128, 8], F32, kind="ExternalInput").ap()
    wf1 = nc.dram_tensor("wf1", [E, E], F32, kind="ExternalInput").ap()
    wa1 = nc.dram_tensor("wa1", [E, E], F32, kind="ExternalInput").ap()
    wa2 = nc.dram_tensor("wa2", [E, E], F32, kind="ExternalInput").ap()
    fbias = nc.dram_tensor("fbias", [128, 8], F32, kind="ExternalInput").ap()
    identb = nc.dram_tensor("identb", [128, 128], BF16, kind="ExternalInput").ap()
    identf = nc.dram_tensor("identf", [128, 128], F32, kind="ExternalInput").ap()
    x4r = nc.dram_tensor("x4r", [TPC, E], F32, kind="ExternalOutput").ap()

    with ExitStack() as stk:
        sb = lambda n, s, d=F32: stk.enter_context(nc.sbuf_tensor(n, s, d))
        ps = lambda n, s, d=F32: stk.enter_context(nc.psum_tensor(n, s, d))
        sem = lambda n: stk.enter_context(nc.semaphore(n))

        gwhf_sb = sb("gwhf_sb", [128, 8 * E], BF16)
        gwif_sb = sb("gwif_sb", [128, 8 * E], BF16)
        wX = sb("wX", [128, 8 * E], BF16)
        gb_sb = sb("gb_sb", [128, 8])
        fb_sb = sb("fb_sb", [128, 8])
        idb = sb("idb", [128, 128], BF16)
        idf = sb("idf", [128, 128])
        slabc = sb("slabc", [128, CH * 32], BF16)
        h2Tc = sb("h2Tc", [128, 8 * 512], BF16)
        xnb = sb("xnb", [128, 8 * 512], BF16)
        xnf = sb("xnf", [128, 8 * 512])
        zaf = sb("zaf", [128, 8 * 512])
        uC = sb("uC", [128, 8 * 512])
        gC = sb("gC", [128, 8 * 512])
        x3b = sb("x3b", [128, 4 * E])
        rowsb = sb("rowsb", [128, E])
        lntile = sb("lntile", [128, E])
        lnbt = sb("lnbt", [128, E], BF16)
        ln4T = sb("ln4T", [128, 8 * 512], BF16)
        fA = sb("fA", [128, 8 * 512], BF16)
        fB = sb("fB", [128, 8 * 512], BF16)
        ftile = sb("ftile", [128, 8 * 512])
        kit = LNKit(nc, stk, "ln")

        pY = [ps("pY%d" % p, [128, 512]) for p in range(2)]
        pTb = [ps("pTb%d" % p, [128, 128], BF16) for p in range(2)]
        pTf = [ps("pTf%d" % p, [128, 128]) for p in range(2)]

        ld = sem("ld"); sT = sem("sT"); sV = sem("sV")
        sA = sem("sA"); sO = sem("sO")
        em = Em(nc)

        for wsb, whbm in [(gwhf_sb, gWhfT), (gwif_sb, gWifT)]:
            for k in range(8):
                em.do("gpsimd", lambda Eg, w=wsb, h=whbm, k=k: Eg.dma_start(
                    out=w[:, k * E:(k + 1) * E], in_=h[ts(k), :]), inc=(ld, 16))
        em.do("sync", lambda Eg: Eg.dma_start(out=gb_sb[:], in_=gbhf), inc=(ld, 16))
        em.do("sync", lambda Eg: Eg.dma_start(out=fb_sb[:], in_=fbias), inc=(ld, 16))
        em.do("sync", lambda Eg: Eg.dma_start(out=idb[:], in_=identb), inc=(ld, 16))
        em.do("sync", lambda Eg: Eg.dma_start(out=idf[:], in_=identf), inc=(ld, 16))
        LD_RES = em.count(ld)
        em.wait("tensor", ld, LD_RES)
        em.wait("scalar", ld, LD_RES)
        em.wait("vector", ld, LD_RES)

        py_free = [None, None]
        ptb_free = [None, None]
        ptf_free = [None, None]
        wx_free = None
        par = 0
        tpar = 0

        def mm8(lhs_sb, j, rhs_sb, rhs_base, extra=None, n2=8):
            nonlocal par
            if py_free[par] is not None:
                em.wait("tensor", *py_free[par])
            if extra is not None:
                for k in range(8):
                    em.do("tensor", lambda Eg, k=k, j=j, l=lhs_sb, r=rhs_sb,
                          rb=rhs_base, par=par: Eg.matmul(
                              pY[par][:], l[:, k * E + j * 128: k * E + (j + 1) * 128],
                              r[:, k * 512 + rb: k * 512 + rb + 512],
                              start=(k == 0), stop=False))
                l2, r2 = extra
                for k in range(8):
                    em.do("tensor", lambda Eg, k=k, j=j, l=l2, r=r2,
                          rb=rhs_base, par=par: Eg.matmul(
                              pY[par][:], l[:, k * E + j * 128: k * E + (j + 1) * 128],
                              r[:, k * 512 + rb: k * 512 + rb + 512],
                              start=False, stop=(k == 7)),
                          inc=((sT, 1) if k == 7 else None))
            else:
                for k in range(8):
                    em.do("tensor", lambda Eg, k=k, j=j, l=lhs_sb, r=rhs_sb,
                          rb=rhs_base, par=par: Eg.matmul(
                              pY[par][:], l[:, k * E + j * 128: k * E + (j + 1) * 128],
                              r[:, k * 512 + rb: k * 512 + rb + 512],
                              start=(k == 0), stop=(k == 7)),
                          inc=((sT, 1) if k == 7 else None))
            return em.count(sT)

        def transp(src_fn, dst_fn, n, fp32):
            nonlocal tpar
            pT_ = pTf if fp32 else pTb
            ptfree = ptf_free if fp32 else ptb_free
            ident = idf if fp32 else idb
            for j in range(n):
                q = tpar
                if ptfree[q] is not None:
                    em.wait("tensor", *ptfree[q])
                em.do("tensor", lambda Eg, j=j, q=q, ident=ident, pT_=pT_:
                      Eg.transpose(pT_[q][:], src_fn(j), ident[:]), inc=(sT, 1))
                em.wait("vector", sT)
                em.do("vector", lambda Eg, j=j, q=q, pT_=pT_: Eg.tensor_copy(
                    dst_fn(j), pT_[q][:]), inc=(sV, 1))
                ptfree[q] = (sV, em.count(sV))
                tpar ^= 1

        def load_wX(whbm):
            nonlocal wx_free
            if wx_free is not None:
                em.wait("gpsimd", *wx_free)
                wx_free = None
            for k in range(8):
                em.do("gpsimd", lambda Eg, h=whbm, k=k: Eg.dma_start(
                    out=wX[:, k * E:(k + 1) * E], in_=h[ts(k), :]), inc=(ld, 16))
            return em.count(ld)

        for cc in range(4):
            em.wait("sync", sV)
            em.wait("sync", sT)
            em.wait("sync", sO)
            em.do("sync", lambda Eg, cc=cc: Eg.dma_start(
                out=slabc[:], in_=h2c[:, cc * CH * 32:(cc + 1) * CH * 32]),
                inc=(ld, 16))
            for j in range(8):
                em.do("sync", lambda Eg, j=j, cc=cc: Eg.dma_start(
                    out=xnb[:, j * 512:(j + 1) * 512],
                    in_=xnTbi[ts(j), cc * 512:(cc + 1) * 512]), inc=(ld, 16))
                em.do("sync", lambda Eg, j=j, cc=cc: Eg.dma_start(
                    out=xnf[:, j * 512:(j + 1) * 512],
                    in_=xnTfi[ts(j), cc * 512:(cc + 1) * 512]), inc=(ld, 16))
                em.do("sync", lambda Eg, j=j, cc=cc: Eg.dma_start(
                    out=zaf[:, j * 512:(j + 1) * 512],
                    in_=zaTi[ts(j), cc * 512:(cc + 1) * 512]), inc=(ld, 16))
            for tt in range(4):
                em.do("sync", lambda Eg, tt=tt, cc=cc: Eg.dma_start(
                    out=x3b[:, tt * E:(tt + 1) * E],
                    in_=x2rc[ts(cc * 4 + tt), :]), inc=(ld, 16))
            CHUNK_LD = em.count(ld)
            wxl = load_wX(gWiT)
            em.wait("vector", ld, CHUNK_LD)
            em.wait("vector", sT)
            for j in range(8):
                for b in range(4):
                    em.do("vector", lambda Eg, j=j, b=b: Eg.tensor_copy(
                        dap(h2Tc, j * 512 + b, 512 * 8, 4, CH),
                        dap(slabc, 4 * j + b, CH * 32, 32, CH)),
                        inc=((sV, 1) if (j, b) == (7, 3) else None))
            UNPACK = em.count(sV)
            em.wait("tensor", sV, UNPACK)
            em.wait("tensor", ld, max(CHUNK_LD, wxl))
            for j in range(8):
                ms = mm8(wX, j, h2Tc, 0)
                em.wait("scalar", sT, ms)
                em.wait("scalar", sV)
                em.do("scalar", lambda Eg, j=j, par=par: Eg.activation(
                    rowsb[:, 0:512], pY[par][:], AF.Tanh))
                em.do("scalar", lambda Eg, j=j: Eg.activation(
                    uC[:, j * 512:(j + 1) * 512], rowsb[:, 0:512], AF.Tanh),
                    inc=(sA, 1))
                py_free[par] = (sA, em.count(sA))
                par ^= 1
                ms = mm8(gwhf_sb, j, xnb, 0, extra=(gwif_sb, h2Tc))
                em.wait("scalar", sT, ms)
                em.do("scalar", lambda Eg, j=j, par=par: Eg.activation(
                    gC[:, j * 512:(j + 1) * 512], pY[par][:], AF.Sigmoid,
                    bias=gb_sb[:, j:j + 1]), inc=(sA, 1))
                py_free[par] = (sA, em.count(sA))
                par ^= 1
            wx_free = (sT, em.count(sT))
            wxl = load_wX(wf1)
            em.wait("vector", sA)
            for j in range(8):
                sl = slice(j * 512, (j + 1) * 512)
                em.do("vector", lambda Eg, sl=sl: Eg.tensor_sub(
                    uC[:, sl], uC[:, sl], xnf[:, sl]))
                em.do("vector", lambda Eg, sl=sl: Eg.tensor_mul(
                    uC[:, sl], gC[:, sl], uC[:, sl]))
                em.do("vector", lambda Eg, sl=sl: Eg.tensor_add(
                    uC[:, sl], uC[:, sl], xnf[:, sl]))
                em.do("vector", lambda Eg, sl=sl: Eg.tensor_add(
                    uC[:, sl], uC[:, sl], zaf[:, sl]), inc=(sV, 1))
            CELL = em.count(sV)
            em.wait("tensor", sV, CELL)
            for tt in range(4):
                transp(lambda j, tt=tt: uC[:, j * 512 + tt * 128:
                                           j * 512 + tt * 128 + 128],
                       lambda j: rowsb[:, ts(j)], 8, fp32=True)
                em.wait("scalar", sV)
                emit_ln(em, kit, sA, sV, 0, rowsb[:], lntile[:])
                x3t = x3b[:, tt * E:(tt + 1) * E]
                em.do("vector", lambda Eg, x3t=x3t: Eg.tensor_add(
                    x3t, x3t, lntile[:]), inc=(sV, 1))
                em.wait("scalar", sV)
                em.wait("vector", sT)
                emit_ln(em, kit, sA, sV, 0, x3t, lntile[:])
                em.do("vector", lambda Eg: Eg.tensor_copy(lnbt[:], lntile[:]),
                      inc=(sV, 1))
                em.wait("tensor", sV)
                transp(lambda j: lnbt[:, ts(j)],
                       lambda j, tt=tt: ln4T[:, j * 512 + tt * 128:
                                             j * 512 + tt * 128 + 128],
                       8, fp32=False)
            em.wait("tensor", ld, wxl)
            em.wait("tensor", sV)
            for j in range(8):
                ms = mm8(wX, j, ln4T, 0)
                em.wait("scalar", sT, ms)
                em.do("scalar", lambda Eg, j=j, par=par: Eg.activation(
                    fA[:, j * 512:(j + 1) * 512], pY[par][:], AF.Copy),
                    inc=(sA, 1))
                py_free[par] = (sA, em.count(sA))
                par ^= 1
            wx_free = (sT, em.count(sT))
            wxl = load_wX(wa1)
            em.wait("tensor", ld, wxl)
            em.wait("tensor", sA)
            for j in range(8):
                ms = mm8(wX, j, fA, 0)
                em.wait("scalar", sT, ms)
                em.do("scalar", lambda Eg, j=j, par=par: Eg.activation(
                    fB[:, j * 512:(j + 1) * 512], pY[par][:], AF.Copy),
                    inc=(sA, 1))
                py_free[par] = (sA, em.count(sA))
                par ^= 1
            wx_free = (sT, em.count(sT))
            wxl = load_wX(wa2)
            em.wait("tensor", ld, wxl)
            em.wait("tensor", sA)
            for j in range(8):
                ms = mm8(wX, j, fB, 0)
                em.wait("scalar", sT, ms)
                em.do("scalar", lambda Eg, j=j, par=par: Eg.activation(
                    ftile[:, j * 512:(j + 1) * 512], pY[par][:], AF.Tanh,
                    bias=fb_sb[:, j:j + 1]), inc=(sA, 1))
                py_free[par] = (sA, em.count(sA))
                par ^= 1
            wx_free = (sT, em.count(sT))
            em.wait("tensor", sA)
            for tt in range(4):
                transp(lambda j, tt=tt: ftile[:, j * 512 + tt * 128:
                                              j * 512 + tt * 128 + 128],
                       lambda j: rowsb[:, ts(j)], 8, fp32=True)
                em.wait("scalar", sV)
                emit_ln(em, kit, sA, sV, 0, rowsb[:], lntile[:])
                x3t = x3b[:, tt * E:(tt + 1) * E]
                em.do("vector", lambda Eg, x3t=x3t: Eg.tensor_add(
                    x3t, x3t, lntile[:]), inc=(sV, 1))
                em.wait("gpsimd", sV)
                em.do("gpsimd", lambda Eg, cc=cc, tt=tt, x3t=x3t: Eg.dma_start(
                    out=x4r[ts(cc * 4 + tt), :], in_=x3t), inc=(sO, 16))

        em.wait("gpsimd", sO, em.count(sO))
        with nc.Block() as block:
            em.emit(block)
    return nc



_BUILD = {}


def _get_nc(name, fn):
    if name not in _BUILD:
        _BUILD[name] = fn()
    return _BUILD[name]


def _pp(vec):
    return np.ascontiguousarray(np.asarray(vec, np.float32).reshape(8, 128).T)


def _bf(a):
    return np.ascontiguousarray(np.asarray(a).astype(BF))


def _f32(a):
    return np.ascontiguousarray(np.asarray(a, np.float32))


def _rope_tables():
    inv = 1.0 / (10000.0 ** (np.arange(0, D, 2, dtype=np.float64) / D))
    pos = np.arange(S, dtype=np.float64)
    freqs = np.outer(pos, inv)
    emb = np.concatenate([freqs, freqs], axis=1)
    cosd = np.cos(emb).T.astype(np.float32)
    sind = np.sin(emb).T.astype(np.float32)
    sgn = np.where(np.arange(D) % 2 == 0, -1.0, 1.0).astype(np.float32)
    sins = sind * sgn[:, None]
    cos128 = np.tile(cosd, (2, 1))
    sin128 = np.tile(sins, (2, 1))
    return cos128, sin128


def kernel(**inp):
    g = lambda k: np.asarray(inp[k])
    x = _f32(g("L0_attn_x"))
    Wq, bq = _f32(g("L0_attn_Wq")), _f32(g("L0_attn_bq"))
    Wk, bk = _f32(g("L0_attn_Wk")), _f32(g("L0_attn_bk"))
    Wv, bv = _f32(g("L0_attn_Wv")), _f32(g("L0_attn_bv"))
    Wo, bo = _f32(g("L0_attn_Wo")), _f32(g("L0_attn_bo"))
    W0 = _f32(g("L0_ffn_weight"))
    gamma = _f32(g("L1_attn_gamma"))
    gWi, gWif, gWhf = (_f32(g("L1_attn_gWi")), _f32(g("L1_attn_gWif")),
                       _f32(g("L1_attn_gWhf")))
    gbhf = _f32(g("L1_attn_gbhf"))
    lWi, lWif, lWhf = (_f32(g("L1_attn_lWi")), _f32(g("L1_attn_lWif")),
                       _f32(g("L1_attn_lWhf")))
    lbhf = _f32(g("L1_attn_lbhf"))
    lih = _f32(g("L1_attn_l_init_hidden"))
    Wf1 = _f32(g("L1_ffn_weight"))
    Wa1 = _f32(g("L1_ffn_aux_weight1"))
    Wa2 = _f32(g("L1_ffn_aux_weight2"))
    fbias = _f32(g("L1_ffn_bias"))

    idf = np.eye(128, dtype=np.float32)
    idb = _bf(idf)
    psw = np.zeros((128, 128), dtype=np.float32)
    for i2 in range(64):
        psw[2 * i2, 2 * i2 + 1] = 1.0
        psw[2 * i2 + 1, 2 * i2] = 1.0
    pswb = _bf(psw)
    cos128, sin128 = _rope_tables()

    cores = list(range(NC))

    ncA = _get_nc("A", build_A)
    wqT, wkT, wvT = (_f32(Wq.T), _f32(Wk.T), _f32(Wv.T))
    imsA = []
    for c in cores:
        r = slice(SPC * c, SPC * (c + 1))
        xr = np.ascontiguousarray(
            np.concatenate([x[b, r] for b in range(B)], axis=0))
        imsA.append(dict(
            xr=xr, wqT=wqT, wkT=wkT, wvT=wvT,
            bqv=_pp(bq), bkv=_pp(bk), bvv=_f32(bv[None, :]),
            ropeC=_bf(np.tile(cos128[:, r], (1, B))),
            ropeS=_bf(np.tile(sin128[:, r], (1, B))),
            pswap=pswb, identb=idb))
    resA = run_bass_kernel_spmd(ncA, imsA, core_ids=cores)
    qT = np.empty((B, E, S), dtype=BF)
    kT = np.empty((B, E, S), dtype=BF)
    vR = np.empty((B, S, E), dtype=BF)
    for c in cores:
        r = slice(SPC * c, SPC * (c + 1))
        ra = resA.results[c]
        for b in range(B):
            qT[b][:, r] = np.asarray(ra["oqT"])[:, b * SPC:(b + 1) * SPC]
            kT[b][:, r] = np.asarray(ra["okT"])[:, b * SPC:(b + 1) * SPC]
            vR[b][r, :] = np.asarray(ra["ov"])[b * SPC:(b + 1) * SPC, :]

    ncB = _get_nc("B", build_B)
    imsB = []
    for c in cores:
        b, hh = c // 2, c % 2
        imsB.append(dict(
            qTh=np.ascontiguousarray(qT[b][hh * 512:(hh + 1) * 512, :]),
            kTh=np.ascontiguousarray(kT[b][hh * 512:(hh + 1) * 512, :]),
            vh=np.ascontiguousarray(vR[b][:, hh * 512:(hh + 1) * 512])))
    resB = run_bass_kernel_spmd(ncB, imsB, core_ids=cores)
    oT = np.empty((B, E, S), dtype=BF)
    for c in cores:
        b, hh = c // 2, c % 2
        oT[b][hh * 512:(hh + 1) * 512, :] = np.asarray(resB.results[c]["oTh"])

    ncCD = _get_nc("CD", build_CD)
    imsCD = []
    for c in cores:
        r = slice(SPC * c, SPC * (c + 1))
        otc = np.ascontiguousarray(
            oT[:, :, r].transpose(1, 2, 0).reshape(E, TPC))
        xrc = np.ascontiguousarray(
            x[:, r, :].transpose(1, 0, 2).reshape(TPC, E))
        imsCD.append(dict(
            otc=otc, xrc=xrc, woT=_f32(Wo.T), bo=_f32(bo[None, :]),
            w0T=_f32(W0.T), wi1T=_f32(lWi[0].T), wif1T=_f32(lWif[0].T),
            bhf1=_pp(lbhf[0]), gamv=_pp(gamma), identb=idb, identf=idf))
    resCD = run_bass_kernel_spmd(ncCD, imsCD, core_ids=cores)
    U1 = np.concatenate(
        [np.asarray(resCD.results[c]["u1c"]) for c in cores], axis=1)
    C1 = np.concatenate(
        [np.asarray(resCD.results[c]["c1c"]) for c in cores], axis=1)

    def h0pack(v):
        h0 = v.reshape(8, 128).T
        return np.ascontiguousarray(
            np.repeat(h0[:, :, None], 4, axis=2).reshape(128, 32))

    ncS = _get_nc("S", build_scan)
    imE = dict(U=np.ascontiguousarray(U1), C=np.ascontiguousarray(C1),
               whfT=_f32(lWhf[0].T), h0p=h0pack(lih[0]))
    resE = run_bass_kernel_spmd(ncS, [imE], core_ids=[0])
    H1 = np.asarray(resE.results[0]["Hs"])

    ncF = _get_nc("F", build_F)
    imsF = []
    for c in cores:
        sl = slice(32 * SPC * c, 32 * SPC * (c + 1))
        imsF.append(dict(h1c=np.ascontiguousarray(H1[:, sl]),
                         wi2T=_f32(lWi[1].T), wif2T=_f32(lWif[1].T),
                         bhf2=_pp(lbhf[1])))
    resF = run_bass_kernel_spmd(ncF, imsF, core_ids=cores)
    U2 = np.concatenate(
        [np.asarray(resF.results[c]["u2c"]) for c in cores], axis=1)
    C2 = np.concatenate(
        [np.asarray(resF.results[c]["c2c"]) for c in cores], axis=1)

    imG = dict(U=np.ascontiguousarray(U2), C=np.ascontiguousarray(C2),
               whfT=_f32(lWhf[1].T), h0p=h0pack(lih[1]))
    resG = run_bass_kernel_spmd(ncS, [imG], core_ids=[0])
    H2 = np.asarray(resG.results[0]["Hs"])

    ncH = _get_nc("H", build_H)
    imsH = []
    for c in cores:
        sl = slice(32 * SPC * c, 32 * SPC * (c + 1))
        rcd = resCD.results[c]
        imsH.append(dict(
            h2c=np.ascontiguousarray(H2[:, sl]),
            xnTbi=np.asarray(rcd["xnTb"]), xnTfi=np.asarray(rcd["xnTf"]),
            zaTi=np.asarray(rcd["zaT"]), x2rc=np.asarray(rcd["x2r"]),
            gWiT=_f32(gWi.T), gWifT=_f32(gWif.T), gWhfT=_f32(gWhf.T),
            gbhf=_pp(gbhf), wf1=Wf1, wa1=Wa1, wa2=Wa2, fbias=_pp(fbias),
            identb=idb, identf=idf))
    resH = run_bass_kernel_spmd(ncH, imsH, core_ids=cores)

    out = np.empty((B, S, E), dtype=np.float32)
    for c in cores:
        r = slice(SPC * c, SPC * (c + 1))
        x4 = np.asarray(resH.results[c]["x4r"]).reshape(SPC, B, E)
        out[:, r, :] = x4.transpose(1, 0, 2)
    return out
